# version 22
# baseline (speedup 1.0000x reference)
import numpy as np
from contextlib import ExitStack

import concourse.bass as bass
import concourse.mybir as mybir
from concourse.bass_utils import run_bass_kernel_spmd

F32 = mybir.dt.float32
BF16 = mybir.dt.bfloat16
AF = mybir.ActivationFunctionType
ALU = mybir.AluOpType
AX = mybir.AxisListType

FMAX = float(np.finfo(np.float32).max)
B_SHIFT = 25.0
EXP_SCALE = 0.08838834764831845
DEAD = 30000.0

D = 2048
S = 2048
H = 16
DH = 128
INTER = 5632
NI = 44
NF = 16
NT = 16
TOWN = 1024
BLEND = True
DEBUG_SPILL = False
NCORE = 8


class Ctr:
    def __init__(self):
        self.v = 0

    def inc(self, n=1):
        self.v += n
        return self.v


class Plan:

    def __init__(self):
        self.ops = {k: [] for k in ("pe", "act", "dve", "gp", "sy")}

    def add(self, eng, emit, waits=(), incs=()):
        self.ops[eng].append((list(waits), emit, list(incs)))


class EngState:

    def __init__(self):
        self.waited = {}

    def run(self, e, ops):
        for waits, emit, incs in ops:
            for sem, v in waits:
                if callable(v):
                    v = v()
                if v is None:
                    continue
                k = id(sem)
                if self.waited.get(k, -1) >= v:
                    continue
                e.wait_ge(sem, v)
                self.waited[k] = v
            ins = emit(e)
            for sem, n in incs:
                ins = ins.then_inc(sem, n)


def build_nc():
    nc = bass.Bass("TRN2", target_bir_lowering=False)

    xfm = nc.dram_tensor("xfm", [D, S], F32, kind="ExternalInput").ap()
    wkp = nc.dram_tensor("wkp", [H, 128, 2048], F32, kind="ExternalInput").ap()
    wqp = nc.dram_tensor("wqp", [H, 128, 2048], F32, kind="ExternalInput").ap()
    wvp = nc.dram_tensor("wvp", [4, 128, 8192], F32, kind="ExternalInput").ap()
    wop = nc.dram_tensor("wop", [16, 128, 2048], F32, kind="ExternalInput").ap()
    gwp = nc.dram_tensor("gwp", [NI, 128, 2048], F32, kind="ExternalInput").ap()
    uwp = nc.dram_tensor("uwp", [NI, 128, 2048], F32, kind="ExternalInput").ap()
    dwp = nc.dram_tensor("dwp", [16, 128, INTER], F32, kind="ExternalInput").ap()
    gsp = nc.dram_tensor("gsp", [128, NI], F32, kind="ExternalInput").ap()
    usp = nc.dram_tensor("usp", [128, NI], F32, kind="ExternalInput").ap()
    dsp = nc.dram_tensor("dsp", [128, 16], F32, kind="ExternalInput").ap()
    maskc = nc.dram_tensor("maskc", [128, 896], F32, kind="ExternalInput").ap()
    ident = nc.dram_tensor("ident", [128, 128], F32, kind="ExternalInput").ap()
    pbl = nc.dram_tensor("pbl", [128, 6], F32, kind="ExternalInput").ap()

    out = nc.dram_tensor("out", [D, TOWN], F32, kind="ExternalOutput").ap()
    spill_kind = {"kind": "ExternalOutput"} if DEBUG_SPILL else {}
    spill = nc.dram_tensor("spill_attn", [D, TOWN], F32, **spill_kind).ap()
    if DEBUG_SPILL:
        hdump = nc.dram_tensor("hdump", [128, NI * TOWN], BF16, kind="ExternalOutput").ap()

    est = {k: EngState() for k in ("pe", "act", "dve", "gp", "sy")}

    outer = ExitStack()
    with outer:
        sPE = outer.enter_context(nc.semaphore("sPE"))
        sAC = outer.enter_context(nc.semaphore("sAC"))
        sVE = outer.enter_context(nc.semaphore("sVE"))
        sST = outer.enter_context(nc.semaphore("sST"))
        sPL = outer.enter_context(nc.semaphore("sPL"))
        cPE, cAC, cVE, cST, cPL = (Ctr() for _ in range(5))
        def dsem(name):
            sem = outer.enter_context(nc.semaphore(name))
            return [sem, Ctr()]

        semC = dsem("semC")
        semXF = [dsem(f"semXF{i}") for i in range(8)]
        semWV = dsem("semWV")
        semWK = [dsem("semWK0"), dsem("semWK1")]
        semWQ = [dsem("semWQ0"), dsem("semWQ1")]
        semWO = [dsem("semWO0"), dsem("semWO1")]
        semGW = [dsem("semGW0"), dsem("semGW1")]
        semUW = [dsem("semUW0"), dsem("semUW1")]
        semDW = [dsem("semDW0"), dsem("semDW1")]
        semSP = dsem("semSP")
        semSI = [dsem("semSI0"), dsem("semSI1")]
        semXR = [dsem("semXR0"), dsem("semXR1")]
        semOUT = [dsem("semOUT0"), dsem("semOUT1")]

        sb = lambda *a: outer.enter_context(nc.sbuf_tensor(*a))

        xn = sb("xn", [128, NF * S], BF16)
        consts = sb("consts", [128, 8], F32)
        ones_bf = sb("ones_bf", [128, 1], BF16)
        ones_row = sb("ones_row", [1, 128], F32)
        ones_row_bf = sb("ones_row_bf", [1, 128], BF16)
        maskcb = sb("maskcb", [128, 896], BF16)
        identb = sb("identb", [128, 128], BF16)
        gsb = sb("gsb", [128, NI], F32)
        usb = sb("usb", [128, NI], F32)
        dsb = sb("dsb", [128, 16], F32)
        pbl_sb = sb("pbl_sb", [128, 6], F32)
        colvT2 = sb("colvT2", [128, 32], F32)
        colvrow = sb("colvrow", [1, 512], F32)
        btmp2 = sb("btmp2", [128, 2], F32)

        def run_block(plan):
            with nc.Block(no_gpsimd_drain=True) as block:
                @block.tensor
                def _(e):
                    est["pe"].run(e, plan.ops["pe"])

                @block.scalar
                def _(e):
                    est["act"].run(e, plan.ops["act"])

                @block.vector
                def _(e):
                    est["dve"].run(e, plan.ops["dve"])

                @block.gpsimd
                def _(e):
                    est["gp"].run(e, plan.ops["gp"])

                @block.sync
                def _(e):
                    est["sy"].run(e, plan.ops["sy"])

        att_stack = ExitStack()
        att_stack.__enter__()
        asb = lambda *a: att_stack.enter_context(nc.sbuf_tensor(*a))
        oall = asb("oall", [128, H * TOWN], BF16)
        vtm = asb("vtm", [128, NT * 512], BF16)
        wvbuf = asb("wvbuf", [128, 8192], BF16)
        wo_t = {}

        ln_stack = ExitStack()
        if True:
            lsb = lambda *a: ln_stack.enter_context(nc.sbuf_tensor(*a))
            xfmbuf = lsb("xfmbuf", [128, 2 * S], F32)
            xcb = lsb("xcb", [128, 2 * S], BF16)
            rowA = lsb("rowA", [1, S], F32)
            rowB = lsb("rowB", [1, S], F32)
            rowD = lsb("rowD", [1, S], F32)
            rowC = rowD
            rowE = rowA
            rbc = lsb("rbc", [128, S], F32)
            mrbc = lsb("mrbc", [128, S], F32)
            tmpnD = lsb("tmpnD", [128, 1024], F32)
            tmpnP = lsb("tmpnP", [128, 1024], F32)

            pl = Plan()
            pl.add("gp", (lambda e: e.dma_start(out=wvbuf[:], in_=wvp[0])),
                   incs=[(semWV[0], 16)])
            wv_t = {0: (semWV[0], semWV[1].inc(16))}
            for dst, src in ((gsb[:], gsp), (usb[:], usp),
                             (dsb[:], dsp), (pbl_sb[:], pbl)):
                pl.add("sy", (lambda e, dst=dst, src=src: e.dma_start(out=dst, in_=src)),
                       incs=[(semC[0], 16)]); semC[1].inc(16)
            pl.add("gp", (lambda e: e.dma_start(out=maskcb[:], in_=maskc)),
                   incs=[(semC[0], 16)]); semC[1].inc(16)
            pl.add("gp", (lambda e: e.dma_start(out=identb[:], in_=ident)),
                   incs=[(semC[0], 16)]); semC[1].inc(16)
            v_const_dma = (semC[0], semC[1].v)
            pl.add("dve", lambda e: e.memset(consts[:, 0:1], -B_SHIFT))
            pl.add("dve", lambda e: e.memset(consts[:, 1:2], EXP_SCALE))
            pl.add("dve", lambda e: e.memset(consts[:, 2:3], 1.0))
            pl.add("dve", lambda e: e.memset(consts[:, 3:4], 1e-5))
            pl.add("dve", lambda e: e.memset(consts[:, 4:5], -0.5))
            pl.add("dve", lambda e: e.memset(ones_bf[:], 1.0))
            pl.add("dve", lambda e: e.memset(colvT2[:], 0.0))
            pl.add("dve", lambda e: e.memset(ones_row_bf[:], 1.0))
            pl.add("dve", lambda e: e.memset(ones_row[:], 1.0), incs=[(sVE, 1)]); cVE.inc()
            v_const_ms = cVE.v

            st_stack = ExitStack()
            s1 = st_stack.enter_context(nc.psum_tensor("s1", [1, S], F32))
            s2 = st_stack.enter_context(nc.psum_tensor("s2", [1, S], F32))
            dma_done = {}
            cast_done = {}
            sq_done = {}
            pemm_done = {}
            for fc in range(NF):
                sl = fc % 2
                w = []
                if fc >= 2:
                    w.append((sAC, sq_done[fc - 2]))
                pl.add("sy", (lambda e, sl=sl, fc=fc: e.dma_start(
                    out=xfmbuf[:, sl * S:(sl + 1) * S],
                    in_=xfm[fc * 128:(fc + 1) * 128, :])), waits=w,
                    incs=[(semXF[sl][0], 16)])
                dma_done[fc] = (semXF[sl][0], semXF[sl][1].inc(16))

                w = [dma_done[fc]]
                if fc >= 2:
                    w.append((sPE, pemm_done[fc - 2]))
                pl.add("act", (lambda e, sl=sl: e.activation(
                    xcb[:, sl * S:(sl + 1) * S], xfmbuf[:, sl * S:(sl + 1) * S], AF.Copy)),
                    waits=w, incs=[(sAC, 1)])
                cast_done[fc] = cAC.inc()

                for j in range(4):
                    w = [(sAC, cast_done[fc])] if j == 0 else []
                    if fc == 0 and j == 0:
                        w.append((sVE, v_const_ms))
                    incs = [(sPE, 1)] if j == 3 else []
                    pl.add("pe", (lambda e, sl=sl, fc=fc, j=j: e.matmul(
                        s1[0:1, j * 512:(j + 1) * 512], ones_bf[:],
                        xcb[:, sl * S + j * 512: sl * S + (j + 1) * 512],
                        start=(fc == 0), stop=(fc == NF - 1))), waits=w, incs=incs)
                v_s1 = cPE.inc()
                pl.add("act", (lambda e, sl=sl: e.activation(
                    xcb[:, sl * S:(sl + 1) * S], xcb[:, sl * S:(sl + 1) * S], AF.Square)),
                    waits=[(sPE, v_s1)], incs=[(sAC, 1)])
                sq_done[fc] = cAC.inc()
                for j in range(4):
                    w = [(sAC, sq_done[fc])] if j == 0 else []
                    incs = [(sPE, 1)] if j == 3 else []
                    pl.add("pe", (lambda e, sl=sl, fc=fc, j=j: e.matmul(
                        s2[0:1, j * 512:(j + 1) * 512], ones_bf[:],
                        xcb[:, sl * S + j * 512: sl * S + (j + 1) * 512],
                        start=(fc == 0), stop=(fc == NF - 1))), waits=w, incs=incs)
                pemm_done[fc] = cPE.inc()
            v_stats = cPE.v

            pl.add("dve", lambda e: e.tensor_copy(rowA[:], s1[:]), waits=[(sPE, v_stats)])
            pl.add("dve", lambda e: e.tensor_copy(rowB[:], s2[:]))
            pl.add("dve", lambda e: e.tensor_scalar_mul(rowA[:], rowA[:], 1.0 / D))
            pl.add("dve", lambda e: e.tensor_scalar_mul(rowB[:], rowB[:], 1.0 / D),
                   incs=[(sVE, 1)])
            v_m = cVE.inc()
            pl.add("act", lambda e: e.activation(rowC[:], rowA[:], AF.Square),
                   waits=[(sVE, v_m)], incs=[(sAC, 1)])
            v_m2 = cAC.inc()
            pl.add("dve", lambda e: e.tensor_sub(rowB[:], rowB[:], rowC[:]),
                   waits=[(sAC, v_m2)], incs=[(sVE, 1)])
            v_var = cVE.inc()
            pl.add("act", lambda e: e.activation(rowC[:], rowB[:], AF.Ln,
                                                 bias=consts[0:1, 3:4]),
                   waits=[(sVE, v_var)])
            pl.add("act", lambda e: e.activation(rowD[:], rowC[:], AF.Exp,
                                                 scale=consts[0:1, 4:5]),
                   incs=[(sAC, 1)])
            v_sd = cAC.inc()
            pl.add("dve", lambda e: e.tensor_mul(rowE[:], rowA[:], rowD[:]),
                   waits=[(sAC, v_sd)], incs=[(sVE, 1)])
            v_rows = cVE.inc()

            st_stack.close()

            a1 = ExitStack()
            a1.__enter__()
            bsb = lambda *a: a1.enter_context(nc.sbuf_tensor(*a))
            bps = lambda *a: a1.enter_context(nc.psum_tensor(*a))
            spA = bps("spA", [128, 1024], F32)
            spB = bps("spB", [128, 1024], F32)
            opsT = bps("opsT", [128, 1024], F32)
            lpsA = bps("lpsA", [128, 512], F32)
            lpsB = bps("lpsB", [128, 512], F32)
            mm_banks = [spA[:, 0:512], spA[:, 512:1024],
                        spB[:, 0:512], spB[:, 512:1024]]
            bank_wait = [[] for _ in range(4)]

            for step in range(8):
                src = rowD if step < 4 else rowE
                dst = rbc if step < 4 else mrbc
                j = step % 4
                bk = step % 2
                w = list(bank_wait[bk])
                if step == 0:
                    w.append((sVE, v_rows))
                pl.add("pe", (lambda e, bk=bk, j=j, src=src: e.matmul(
                    mm_banks[bk], ones_row[:],
                    src[0:1, j * 512:(j + 1) * 512], start=True, stop=True)),
                    waits=w, incs=[(sPE, 1)])
                vp = cPE.inc()
                pl.add("dve", (lambda e, bk=bk, j=j, dst=dst: e.tensor_copy(
                    dst[:, j * 512:(j + 1) * 512], mm_banks[bk])),
                    waits=[(sPE, vp)], incs=[(sVE, 1)])
                bank_wait[bk] = [(sVE, cVE.inc())]
            v_bc = cVE.v

            norm_done = {}
            for step in range(4 * NF):
                quarter, fc = step // NF, step % NF
                sl = step % 8
                hb = quarter * 512
                on_pool = (step % 3 == 1)
                eng = "gp" if on_pool else "dve"
                tmpn = tmpnP if on_pool else tmpnD
                sm = semXF[sl]
                w = []
                if step >= 8:
                    w.append(norm_done[step - 8])
                elif step < 2:
                    w.append((sAC, sq_done[NF - 2 + step]))
                else:
                    w.append((sAC, sq_done[NF - 1]))
                pl.add("sy", (lambda e, sl=sl, fc=fc, hb=hb: e.dma_start(
                    out=xfmbuf[:, sl * 512:(sl + 1) * 512],
                    in_=xfm[fc * 128:(fc + 1) * 128, hb:hb + 512])),
                    waits=w, incs=[(sm[0], 16)])
                d = (sm[0], sm[1].inc(16))
                pl.add(eng, (lambda e, sl=sl, hb=hb, tmpn=tmpn: e.tensor_mul(
                    tmpn[:, 0:512], xfmbuf[:, sl * 512:(sl + 1) * 512],
                    rbc[:, hb:hb + 512])), waits=[d, (sVE, v_bc)])
                inc_sem = sPL if on_pool else sVE
                ctr = cPL if on_pool else cVE
                pl.add(eng, (lambda e, sl=sl, fc=fc, hb=hb, tmpn=tmpn: e.tensor_sub(
                    xn[:, fc * S + hb: fc * S + hb + 512], tmpn[:, 0:512],
                    mrbc[:, hb:hb + 512])), incs=[(inc_sem, 1)])
                norm_done[step] = (inc_sem, ctr.inc())
            def _cover(upto):
                lastd = lastp = None
                for t in range(upto):
                    if t % 3 == 1:
                        lastp = norm_done[t]
                    else:
                        lastd = norm_done[t]
                return [x for x in (lastd, lastp) if x is not None]
            v_xn_q = [_cover(NF * (q + 1)) for q in range(4)]
            v_xn = v_xn_q[3]

            ln_stack.close()

            kh = bsb("kh", [128, S], BF16)
            wobuf = bsb("wobuf", [128, 2 * 2048], BF16)
            if True:
                qh = bsb("qh", [128, TOWN], BF16)
                pt = bsb("pt", [128, 3 * 1024], BF16)
                wkbuf = bsb("wkbuf", [128, 2 * 2048], BF16)
                wqbuf = bsb("wqbuf", [128, 2 * 2048], BF16)
                rbc_a = bsb("rbc_a", [128, 2 * 512], F32)
                addv = bsb("addv", [128, 512], F32)
                lrow = bsb("lrow", [1, 2 * 512], F32)
                rrow = bsb("rrow", [1, 2 * 512], F32)
                rrowb = bsb("rrowb", [1, 2 * 512], BF16)

                pair_reg = [spA[:], spB[:]]
                pt_bufs = [pt[:, 0:1024], pt[:, 1024:2048], pt[:, 2048:3072]]
                o_banks = [opsT[:, 0:512], opsT[:, 512:1024]]
                l_banks = [lpsA[0:1, :], lpsB[0:1, :]]
                l_full = [lpsA[:], lpsB[:]]
                rbc_bufs = [rbc_a[:, 0:512], rbc_a[:, 512:1024]]
                lrow_bufs = [lrow[0:1, 0:512], lrow[0:1, 512:1024]]
                rrow_bufs = [rrow[0:1, 0:512], rrow[0:1, 512:1024]]
                rrowb_bufs = [rrowb[0:1, 0:512], rrowb[0:1, 512:1024]]

                pt_wait = [[], [], []]
                o_wait = [[], []]
                l_wait = [[], []]

                pending = []

                def flush_pending():
                    while pending:
                        pending.pop(0)()

                pl.add("dve", (lambda e: e.memset(addv[:], 0.0)), waits=list(v_xn))

                wk_t = {}
                wq_t = {}
                wv_free = {}
                wk_free = {}
                wq_free = {}
                gp_sched = []
                for h in range(H):
                    gp_sched.append(("wk", h))
                    gp_sched.append(("wq", h))
                    if h % 4 == 1 and h // 4 + 1 < 4:
                        gp_sched.append(("wv", h // 4 + 1))

                for kind, idx in gp_sched:
                    if kind == "wv":
                        w = [(sPE, (lambda i=idx: wv_free.get(i - 1)))] if idx >= 1 else []
                        pl.add("gp", (lambda e, idx=idx: e.dma_start(
                            out=wvbuf[:], in_=wvp[idx])), waits=w, incs=[(semWV[0], 16)])
                        wv_t[idx] = (semWV[0], semWV[1].inc(16))
                    elif kind == "wk":
                        w = [(sPE, (lambda i=idx: wk_free.get(i - 2)))] if idx >= 2 else list(v_xn)
                        sm = semWK[idx % 2]
                        pl.add("gp", (lambda e, idx=idx: e.dma_start(
                            out=wkbuf[:, (idx % 2) * 2048:(idx % 2 + 1) * 2048],
                            in_=wkp[idx])), waits=w, incs=[(sm[0], 16)])
                        wk_t[idx] = (sm[0], sm[1].inc(16))
                    else:
                        w = [(sPE, (lambda i=idx: wq_free.get(i - 2)))] if idx >= 2 else list(v_xn)
                        sm = semWQ[idx % 2]
                        pl.add("gp", (lambda e, idx=idx: e.dma_start(
                            out=wqbuf[:, (idx % 2) * 2048:(idx % 2 + 1) * 2048],
                            in_=wqp[idx])), waits=w, incs=[(sm[0], 16)])
                        wq_t[idx] = (sm[0], sm[1].inc(16))

                pp = 0
                uu = 0
                for g in range(4):
                    vtm_evac = {}
                    for tau in range(NT):
                        bk = tau % 4
                        for fc in range(NF):
                            w = []
                            if fc == 0:
                                w = list(bank_wait[bk]) + [wv_t[g]] + \
                                    list(v_xn_q[min(tau // 4, 3)])
                            incs = [(sPE, 1)] if fc == NF - 1 else []
                            pl.add("pe", (lambda e, bk=bk, tau=tau, fc=fc: e.matmul(
                                mm_banks[bk],
                                xn[:, fc * S + tau * 128: fc * S + tau * 128 + 128],
                                wvbuf[:, fc * 512:(fc + 1) * 512],
                                start=(fc == 0), stop=(fc == NF - 1))), waits=w, incs=incs)
                        vmm = cPE.inc()
                        if tau == 0:
                            flush_pending()
                        pl.add("act", (lambda e, bk=bk, tau=tau: e.activation(
                            vtm[:, tau * 512:(tau + 1) * 512], mm_banks[bk], AF.Copy)),
                            waits=[(sPE, vmm)], incs=[(sAC, 1)])
                        vtm_evac[tau] = cAC.inc()
                        bank_wait[bk] = [(sAC, vtm_evac[tau])]
                    wv_free[g] = cPE.v

                    cb = 0
                    for tau in range(NT):
                        w = [(sAC, vtm_evac[tau])]
                        if tau == 0:
                            w += list(l_wait[cb])
                        incs = [(sPE, 1)] if tau == NT - 1 else []
                        pl.add("pe", (lambda e, tau=tau: e.matmul(
                            l_banks[0], ones_bf[:], vtm[:, tau * 512:(tau + 1) * 512],
                            start=(tau == 0), stop=(tau == NT - 1))), waits=w, incs=incs)
                    v_colv = cPE.inc()
                    pl.add("dve", (lambda e, g=g: e.tensor_copy(
                        colvrow[0:1, 0:512], l_banks[0])),
                        waits=[(sPE, v_colv)], incs=[(sVE, 1)])
                    v_cvrow = cVE.inc()
                    l_wait[cb] = [(sVE, v_cvrow)]
                    for j in range(4):
                        h = 4 * g + j
                        bk = j % 4
                        w = list(bank_wait[bk]) + [(sVE, v_cvrow)]
                        pl.add("pe", (lambda e, g=g, j=j, bk=bk: e.matmul(
                            mm_banks[bk][:, 0:1],
                            colvrow[0:1, j * 128:(j + 1) * 128],
                            ones_row[0:1, 0:1], start=True, stop=True)),
                            waits=w, incs=[(sPE, 1)])
                        vp = cPE.inc()
                        pl.add("dve", (lambda e, h=h, bk=bk: e.tensor_mul(
                            colvT2[:, 2 * h + 1:2 * h + 2], mm_banks[bk][:, 0:1], pbl_sb[:, 1:2])),
                            waits=[(sPE, vp), v_const_dma], incs=[(sVE, 1)])
                        bank_wait[bk] = [(sVE, cVE.inc())]

                    for j in range(4):
                        h = 4 * g + j
                        k_evac = {}
                        for blk in range(4):
                            bk = blk
                            for fc in range(NF):
                                w = []
                                if fc == 0:
                                    w = list(bank_wait[bk]) + [wk_t[h]]
                                incs = [(sPE, 1)] if fc == NF - 1 else []
                                pl.add("pe", (lambda e, h=h, bk=bk, blk=blk, fc=fc: e.matmul(
                                    mm_banks[bk],
                                    wkbuf[:, (h % 2) * 2048 + fc * 128:(h % 2) * 2048 + (fc + 1) * 128],
                                    xn[:, fc * S + blk * 512: fc * S + (blk + 1) * 512],
                                    start=(fc == 0), stop=(fc == NF - 1))), waits=w, incs=incs)
                            vmm = cPE.inc()
                            if blk == 1:
                                flush_pending()
                            pl.add("act", (lambda e, bk=bk, blk=blk: e.activation(
                                kh[:, blk * 512:(blk + 1) * 512], mm_banks[bk], AF.Copy)),
                                waits=[(sPE, vmm)], incs=[(sAC, 1)])
                            k_evac[blk] = cAC.inc()
                            bank_wait[bk] = [(sAC, k_evac[blk])]
                        wk_free[h] = cPE.v
                        q_evac = {}
                        for blk in range(2):
                            bk = blk
                            for fc in range(NF):
                                w = []
                                if fc == 0:
                                    w = list(bank_wait[bk]) + [wq_t[h]]
                                incs = [(sPE, 1)] if fc == NF - 1 else []
                                pl.add("pe", (lambda e, h=h, bk=bk, blk=blk, fc=fc: e.matmul(
                                    mm_banks[bk],
                                    wqbuf[:, (h % 2) * 2048 + fc * 128:(h % 2) * 2048 + (fc + 1) * 128],
                                    xn[:, fc * S + blk * 512: fc * S + (blk + 1) * 512],
                                    start=(fc == 0), stop=(fc == NF - 1))), waits=w, incs=incs)
                            vmm = cPE.inc()
                            pl.add("act", (lambda e, bk=bk, blk=blk: e.activation(
                                qh[:, blk * 512:(blk + 1) * 512], mm_banks[bk], AF.Copy)),
                                waits=[(sPE, vmm)], incs=[(sAC, 1)])
                            q_evac[blk] = cAC.inc()
                            bank_wait[bk] = [(sAC, q_evac[blk])]
                        wq_free[h] = cPE.v
                        v_kq_ready = cAC.v

                        for qb in range(2):
                            t0 = 4 * qb
                            pairs = [(15, 14), (13, 12), (11, 10), (9, 8),
                                     (7, 6), (5, 4)]
                            if qb == 0:
                                pairs += [(3, 2), (1, 0)]
                            ou = uu % 2
                            npairs = len(pairs)
                            pair_state = []

                            def plan_S(pi):
                                khi, klo = pairs[pi]
                                pr = (pp + pi) % 2
                                ps3 = (pp + pi) % 3
                                bhi, blo = 2 * pr, 2 * pr + 1
                                for half, kk in ((0, khi), (1, klo)):
                                    bk = bhi if half == 0 else blo
                                    w = list(bank_wait[bk])
                                    if pi == 0 and half == 0:
                                        w.append((sAC, v_kq_ready))
                                    masked = (t0 <= kk <= t0 + 3)
                                    if masked:
                                        off = (kk - t0) * 128
                                        pl.add("pe", (lambda e, bk=bk, off=off: e.matmul(
                                            mm_banks[bk], identb[:],
                                            maskcb[:, 384 - off:896 - off],
                                            start=True, stop=False)),
                                            waits=w + [v_const_dma])
                                        w = []
                                    incs = [(sPE, 1)] if half == 1 else []
                                    pl.add("pe", (lambda e, bk=bk, kk=kk, qb=qb, masked=masked: e.matmul(
                                        mm_banks[bk],
                                        kh[:, kk * 128:(kk + 1) * 128],
                                        qh[:, qb * 512:(qb + 1) * 512],
                                        start=(not masked), stop=True)), waits=w, incs=incs)
                                vmm = cPE.inc()
                                bias = consts[:, 0:1] if khi < 8 else pbl_sb[:, 2:3]
                                w = [(sPE, vmm), v_const_dma] + list(pt_wait[ps3])
                                pl.add("act", (lambda e, ps3=ps3, pr=pr, bias=bias: e.activation(
                                    pt_bufs[ps3], pair_reg[pr], AF.Exp,
                                    bias=bias, scale=consts[:, 1:2])),
                                    waits=w, incs=[(sAC, 1)])
                                vexp = cAC.inc()
                                bank_wait[bhi] = [(sAC, vexp)]
                                bank_wait[blo] = [(sAC, vexp)]
                                pair_state.append((khi, klo, ps3, vexp))

                            def plan_AV(pi):
                                khi, klo, ps3, vexp = pair_state[pi]
                                pr = ps3
                                first = (pi == 0)
                                last = (pi == npairs - 1)
                                w = [(sAC, vexp), (sAC, vtm_evac[NT - 1])]
                                if first:
                                    w += list(o_wait[ou]) + list(l_wait[ou])
                                pl.add("pe", (lambda e, ou=ou, khi=khi, j=j, pr=pr, first=first: e.matmul(
                                    o_banks[ou],
                                    vtm[:, khi * 512 + j * 128: khi * 512 + (j + 1) * 128],
                                    pt_bufs[pr][:, 0:512],
                                    start=first, stop=False)), waits=w)
                                pl.add("pe", (lambda e, ou=ou, klo=klo, j=j, pr=pr, last=last: e.matmul(
                                    o_banks[ou],
                                    vtm[:, klo * 512 + j * 128: klo * 512 + (j + 1) * 128],
                                    pt_bufs[pr][:, 512:1024],
                                    start=False, stop=last)))
                                pl.add("pe", (lambda e, ou=ou, pr=pr, first=first: e.matmul(
                                    l_banks[ou], ones_bf[:], pt_bufs[pr][:, 0:512],
                                    start=first, stop=False)))
                                pl.add("pe", (lambda e, ou=ou, pr=pr, last=last: e.matmul(
                                    l_banks[ou], ones_bf[:], pt_bufs[pr][:, 512:1024],
                                    start=False, stop=last)), incs=[(sPE, 1)])
                                vav = cPE.inc()
                                pt_wait[pr] = [(sPE, vav)]

                            for pi in range(npairs):
                                plan_S(pi)
                                if pi == 3:
                                    flush_pending()
                                if pi >= 2:
                                    plan_AV(pi - 2)
                            plan_AV(npairs - 2)
                            plan_AV(npairs - 1)
                            pp += npairs
                            v_last_av = cPE.v
                            pl.add("dve", (lambda e, ou=ou: e.tensor_scalar_add(
                                lrow_bufs[ou], l_banks[ou], 1e-30)),
                                waits=[(sPE, v_last_av)])
                            pl.add("dve", (lambda e, ou=ou: e.reciprocal(
                                rrow_bufs[ou], lrow_bufs[ou])))
                            pl.add("dve", (lambda e, ou=ou: e.tensor_copy(
                                rrowb_bufs[ou], rrow_bufs[ou])), incs=[(sVE, 1)])
                            v_r = cVE.inc()
                            def mk_epilogue(ou=ou, h=h, qb=qb, v_r=v_r):
                                def ep():
                                    pl.add("pe", (lambda e: e.matmul(
                                        l_full[ou], ones_row_bf[:], rrowb_bufs[ou],
                                        start=True, stop=True)),
                                        waits=[(sVE, v_r)], incs=[(sPE, 1)])
                                    v_bcm = cPE.inc()
                                    pl.add("dve", (lambda e: e.tensor_copy(
                                        rbc_bufs[ou], l_full[ou])), waits=[(sPE, v_bcm)])
                                    pl.add("dve", (lambda e: e.tensor_mul(
                                        oall[:, h * TOWN + qb * 512: h * TOWN + (qb + 1) * 512],
                                        o_banks[ou], rbc_bufs[ou])), incs=[(sVE, 1)])
                                    v_oev = cVE.inc()
                                    o_wait[ou] = [(sVE, v_oev)]
                                    l_wait[ou] = [(sVE, v_oev)]
                                    if qb == 1 and BLEND:
                                        base = h * TOWN + 512
                                        pl.add("dve", (lambda e: e.tensor_copy(
                                            addv[:, 511:512], colvT2[:, 2 * h + 1:2 * h + 2])))
                                        pl.add("dve", (lambda e: e.tensor_add(
                                            oall[:, base:base + 512],
                                            oall[:, base:base + 512], addv[:])),
                                            incs=[(sVE, 1)])
                                        cVE.inc()
                                return ep

                            pending.append(mk_epilogue())
                            uu += 1
                flush_pending()
                for o in range(2):
                    sm = semWO[o % 2]
                    pl.add("gp", (lambda e, o=o: e.dma_start(
                        out=wobuf[:, (o % 2) * 2048:(o % 2 + 1) * 2048],
                        in_=wop[o])), waits=list(v_xn), incs=[(sm[0], 16)])
                    wo_t[o] = (sm[0], sm[1].inc(16))
                run_block(pl)
            a1.close()

            a2 = ExitStack()
            with a2:
                spbuf = a2.enter_context(nc.sbuf_tensor("spbuf", [128, 2 * 512], F32))
                wops = a2.enter_context(nc.psum_tensor("wops", [128, 1024], F32))
                wo_banks = [wops[:, 0:512], wops[:, 512:1024]]
                wo_bank_wait = [[], []]
                wo_free = {}
                pl = Plan()
                for o in range(2, 16):
                    w = [(sPE, (lambda o=o: wo_free.get(o - 2)))]
                    sm = semWO[o % 2]
                    pl.add("gp", (lambda e, o=o: e.dma_start(
                        out=wobuf[:, (o % 2) * 2048:(o % 2 + 1) * 2048],
                        in_=wop[o])), waits=w, incs=[(sm[0], 16)])
                    wo_t[o] = (sm[0], sm[1].inc(16))
                spw_dma = {}
                for o in range(16):
                    for qb in range(2):
                        u = 2 * o + qb
                        bk = u % 2
                        for hh in range(H):
                            w = []
                            if hh == 0:
                                w = list(wo_bank_wait[bk]) + [wo_t[o]]
                            incs = [(sPE, 1)] if hh == H - 1 else []
                            pl.add("pe", (lambda e, o=o, qb=qb, bk=bk, hh=hh: e.matmul(
                                wo_banks[bk],
                                wobuf[:, (o % 2) * 2048 + hh * 128:(o % 2) * 2048 + (hh + 1) * 128],
                                oall[:, hh * TOWN + qb * 512: hh * TOWN + (qb + 1) * 512],
                                start=(hh == 0), stop=(hh == H - 1))), waits=w, incs=incs)
                        vmm = cPE.inc()
                        wo_free[o] = vmm
                        w = [(sPE, vmm)]
                        if u >= 2:
                            w.append(spw_dma[u - 2])
                        pl.add("dve", (lambda e, bk=bk: e.tensor_copy(
                            spbuf[:, bk * 512:(bk + 1) * 512], wo_banks[bk])),
                            waits=w, incs=[(sVE, 1)])
                        vcp = cVE.inc()
                        wo_bank_wait[bk] = [(sVE, vcp)]
                        pl.add("sy", (lambda e, o=o, qb=qb, bk=bk: e.dma_start(
                            out=spill[o * 128:(o + 1) * 128, qb * 512:(qb + 1) * 512],
                            in_=spbuf[:, bk * 512:(bk + 1) * 512])),
                            waits=[(sVE, vcp)], incs=[(semSP[0], 16)])
                        spw_dma[u] = (semSP[0], semSP[1].inc(16))
                v_spill_all = (semSP[0], semSP[1].v)
                run_block(pl)
        att_stack.close()

        ffn_stack = ExitStack()
        with ffn_stack:
            fsb = lambda *a: ffn_stack.enter_context(nc.sbuf_tensor(*a))
            fps = lambda *a: ffn_stack.enter_context(nc.psum_tensor(*a))
            hbuf = fsb("hbuf", [128, NI * TOWN], BF16)
            t1b = fsb("t1b", [128, 2 * 512], BF16)
            t2b = fsb("t2b", [128, 2 * 512], BF16)
            dwbuf = fsb("dwbuf", [128, 2 * INTER], BF16)
            dw_t = {}

            f1 = ExitStack()
            with f1:
                gwbuf = f1.enter_context(nc.sbuf_tensor("gwbuf", [128, 2 * 2048], BF16))
                uwbuf = f1.enter_context(nc.sbuf_tensor("uwbuf", [128, 2 * 2048], BF16))
                gps = f1.enter_context(nc.psum_tensor("gps", [128, 1024], F32))
                ups = f1.enter_context(nc.psum_tensor("ups", [128, 1024], F32))
                g_banks = [gps[:, 0:512], gps[:, 512:1024]]
                u_banks = [ups[:, 0:512], ups[:, 512:1024]]
                g_bw = [[], []]
                u_bw = [[], []]
                t_bw = [[], []]
                gw_t = {}
                uw_t = {}
                gu_free = {}
                pl = Plan()
                for i in range(NI):
                    w = [(sPE, (lambda i=i: gu_free.get(i - 2)))] if i >= 2 else []
                    smg, smu = semGW[i % 2], semUW[i % 2]
                    pl.add("gp", (lambda e, i=i: e.dma_start(
                        out=gwbuf[:, (i % 2) * 2048:(i % 2 + 1) * 2048],
                        in_=gwp[i])), waits=w, incs=[(smg[0], 16)])
                    gw_t[i] = (smg[0], smg[1].inc(16))
                    pl.add("gp", (lambda e, i=i: e.dma_start(
                        out=uwbuf[:, (i % 2) * 2048:(i % 2 + 1) * 2048],
                        in_=uwp[i])), incs=[(smu[0], 16)])
                    uw_t[i] = (smu[0], smu[1].inc(16))
                for i in range(NI):
                    for tb in range(2):
                        u = 2 * i + tb
                        bk = u % 2
                        for fc in range(NF):
                            w = []
                            if fc == 0:
                                w = list(g_bw[bk]) + [gw_t[i]]
                            pl.add("pe", (lambda e, i=i, tb=tb, bk=bk, fc=fc: e.matmul(
                                g_banks[bk],
                                gwbuf[:, (i % 2) * 2048 + fc * 128:(i % 2) * 2048 + (fc + 1) * 128],
                                xn[:, fc * S + tb * 512: fc * S + (tb + 1) * 512],
                                start=(fc == 0), stop=(fc == NF - 1))), waits=w)
                        for fc in range(NF):
                            w = []
                            if fc == 0:
                                w = list(u_bw[bk]) + [uw_t[i]]
                            incs = [(sPE, 1)] if fc == NF - 1 else []
                            pl.add("pe", (lambda e, i=i, tb=tb, bk=bk, fc=fc: e.matmul(
                                u_banks[bk],
                                uwbuf[:, (i % 2) * 2048 + fc * 128:(i % 2) * 2048 + (fc + 1) * 128],
                                xn[:, fc * S + tb * 512: fc * S + (tb + 1) * 512],
                                start=(fc == 0), stop=(fc == NF - 1))), waits=w, incs=incs)
                        vmm = cPE.inc()
                        gu_free[i] = vmm
                        w = [(sPE, vmm), v_const_dma] + list(t_bw[bk])
                        pl.add("act", (lambda e, i=i, bk=bk: e.activation(
                            t1b[:, bk * 512:(bk + 1) * 512], g_banks[bk], AF.Silu,
                            scale=gsb[:, i:i + 1])), waits=w, incs=[(sAC, 1)])
                        cAC.inc()
                        pl.add("act", (lambda e, i=i, bk=bk: e.activation(
                            t2b[:, bk * 512:(bk + 1) * 512], u_banks[bk], AF.Copy,
                            scale=usb[:, i:i + 1])), incs=[(sAC, 1)])
                        vact = cAC.inc()
                        g_bw[bk] = [(sAC, vact)]
                        u_bw[bk] = [(sAC, vact)]
                        pl.add("dve", (lambda e, i=i, tb=tb, bk=bk: e.tensor_mul(
                            hbuf[:, i * TOWN + tb * 512: i * TOWN + (tb + 1) * 512],
                            t1b[:, bk * 512:(bk + 1) * 512],
                            t2b[:, bk * 512:(bk + 1) * 512])),
                            waits=[(sAC, vact)], incs=[(sVE, 1)])
                        t_bw[bk] = [(sVE, cVE.inc())]
                v_h_all = cVE.v
                for o in range(2):
                    sm = semDW[o % 2]
                    pl.add("gp", (lambda e, o=o: e.dma_start(
                        out=dwbuf[:, (o % 2) * INTER:(o % 2 + 1) * INTER],
                        in_=dwp[o])), incs=[(sm[0], 16)])
                    dw_t[o] = (sm[0], sm[1].inc(16))
                if DEBUG_SPILL:
                    pl.add("sy", (lambda e: e.dma_start(out=hdump, in_=hbuf[:])),
                           waits=[(sVE, v_h_all)], incs=[(sST, 16)])
                    cST.inc(16)
                run_block(pl)

            f2 = ExitStack()
            with f2:
                spin = f2.enter_context(nc.sbuf_tensor("spin", [128, 2 * 512], F32))
                xres = f2.enter_context(nc.sbuf_tensor("xres", [128, 2 * 512], F32))
                fbuf = f2.enter_context(nc.sbuf_tensor("fbuf", [128, 2 * 512], F32))
                obuf = f2.enter_context(nc.sbuf_tensor("obuf", [128, 2 * 512], F32))
                dps = f2.enter_context(nc.psum_tensor("dps", [128, 1024], F32))
                d_banks = [dps[:, 0:512], dps[:, 512:1024]]
                d_bw = [[], []]
                f_bw = [[], []]
                o_bw = [[], []]
                si_bw = [[], []]
                xr_bw = [[], []]
                dw_free = {}
                pl = Plan()
                for o in range(2, 16):
                    w = [(sPE, (lambda o=o: dw_free.get(o - 2)))]
                    sm = semDW[o % 2]
                    pl.add("gp", (lambda e, o=o: e.dma_start(
                        out=dwbuf[:, (o % 2) * INTER:(o % 2 + 1) * INTER],
                        in_=dwp[o])), waits=w, incs=[(sm[0], 16)])
                    dw_t[o] = (sm[0], sm[1].inc(16))
                sp_t = {}
                xr_t = {}
                for o in range(16):
                    for tb in range(2):
                        u = 2 * o + tb
                        bk = u % 2
                        w = [v_spill_all] + list(si_bw[bk])
                        smi, smx = semSI[bk], semXR[bk]
                        pl.add("sy", (lambda e, o=o, tb=tb, bk=bk: e.dma_start(
                            out=spin[:, bk * 512:(bk + 1) * 512],
                            in_=spill[o * 128:(o + 1) * 128, tb * 512:(tb + 1) * 512])),
                            waits=w, incs=[(smi[0], 16)])
                        sp_t[u] = (smi[0], smi[1].inc(16))
                        pl.add("sy", (lambda e, o=o, tb=tb, bk=bk: e.dma_start(
                            out=xres[:, bk * 512:(bk + 1) * 512],
                            in_=xfm[o * 128:(o + 1) * 128, tb * 512:(tb + 1) * 512])),
                            waits=list(xr_bw[bk]), incs=[(smx[0], 16)])
                        xr_t[u] = (smx[0], smx[1].inc(16))
                        for ic in range(NI):
                            w = []
                            if ic == 0:
                                w = list(d_bw[bk]) + [dw_t[o], (sVE, v_h_all)]
                            incs = [(sPE, 1)] if ic == NI - 1 else []
                            pl.add("pe", (lambda e, o=o, tb=tb, bk=bk, ic=ic: e.matmul(
                                d_banks[bk],
                                dwbuf[:, (o % 2) * INTER + ic * 128:(o % 2) * INTER + (ic + 1) * 128],
                                hbuf[:, ic * TOWN + tb * 512: ic * TOWN + (tb + 1) * 512],
                                start=(ic == 0), stop=(ic == NI - 1))), waits=w, incs=incs)
                        vmm = cPE.inc()
                        dw_free[o] = vmm
                        w = [(sPE, vmm), v_const_dma] + list(f_bw[bk])
                        pl.add("act", (lambda e, o=o, bk=bk: e.activation(
                            fbuf[:, bk * 512:(bk + 1) * 512], d_banks[bk], AF.Copy,
                            scale=dsb[:, o:o + 1])), waits=w, incs=[(sAC, 1)])
                        vact = cAC.inc()
                        d_bw[bk] = [(sAC, vact)]
                        pl.add("dve", (lambda e, bk=bk: e.tensor_add(
                            fbuf[:, bk * 512:(bk + 1) * 512],
                            fbuf[:, bk * 512:(bk + 1) * 512],
                            spin[:, bk * 512:(bk + 1) * 512])),
                            waits=[(sAC, vact), sp_t[u]], incs=[(sVE, 1)])
                        si_bw[bk] = [(sVE, cVE.inc())]
                        f_bw[bk] = list(si_bw[bk])
                        w = [xr_t[u]] + list(o_bw[bk])
                        pl.add("dve", (lambda e, bk=bk: e.tensor_add(
                            obuf[:, bk * 512:(bk + 1) * 512],
                            fbuf[:, bk * 512:(bk + 1) * 512],
                            xres[:, bk * 512:(bk + 1) * 512])),
                            waits=w, incs=[(sVE, 1)])
                        vo = cVE.inc()
                        f_bw[bk] = [(sVE, vo)]
                        xr_bw[bk] = [(sVE, vo)]
                        smo = semOUT[bk]
                        pl.add("sy", (lambda e, o=o, tb=tb, bk=bk: e.dma_start(
                            out=out[o * 128:(o + 1) * 128, tb * 512:(tb + 1) * 512],
                            in_=obuf[:, bk * 512:(bk + 1) * 512])),
                            waits=[(sVE, vo)], incs=[(smo[0], 16)])
                        o_bw[bk] = [(smo[0], smo[1].inc(16))]
                pl.add("sy", lambda e: e.nop(),
                       waits=[(semOUT[0][0], semOUT[0][1].v),
                              (semOUT[1][0], semOUT[1][1].v)])
                run_block(pl)

    return nc


_NC_CACHE = {}


def _get_nc():
    if "nc" not in _NC_CACHE:
        _NC_CACHE["nc"] = build_nc()
    return _NC_CACHE["nc"]


def kernel(L0_attn_x, L0_attn_Wq, L0_attn_Wk, L0_attn_Wv, L0_attn_Wo,
           L0_ffn_gate_w, L0_ffn_up_w, L0_ffn_down_w,
           L0_ffn_gate_s, L0_ffn_up_s, L0_ffn_down_s):
    x = np.ascontiguousarray(np.asarray(L0_attn_x, dtype=np.float32))
    wq = np.asarray(L0_attn_Wq, dtype=np.float32)
    wk = np.asarray(L0_attn_Wk, dtype=np.float32)
    wv = np.asarray(L0_attn_Wv, dtype=np.float32)
    wo = np.asarray(L0_attn_Wo, dtype=np.float32)
    gw = np.asarray(L0_ffn_gate_w, dtype=np.float32)
    uw = np.asarray(L0_ffn_up_w, dtype=np.float32)
    dw = np.asarray(L0_ffn_down_w, dtype=np.float32)
    gs = np.asarray(L0_ffn_gate_s, dtype=np.float32)
    us = np.asarray(L0_ffn_up_s, dtype=np.float32)
    ds = np.asarray(L0_ffn_down_s, dtype=np.float32)

    wkp = np.ascontiguousarray(wk.reshape(16, 128, 16, 128).transpose(2, 1, 0, 3).reshape(16, 128, 2048))
    wqp = np.ascontiguousarray(wq.reshape(16, 128, 16, 128).transpose(2, 1, 0, 3).reshape(16, 128, 2048))
    wvp = np.ascontiguousarray(wv.reshape(16, 128, 4, 512).transpose(2, 1, 0, 3).reshape(4, 128, 8192))
    wop = np.ascontiguousarray(wo.reshape(16, 128, 16, 128).transpose(2, 1, 0, 3).reshape(16, 128, 2048))
    gwp = np.ascontiguousarray(gw.reshape(44, 128, 16, 128).transpose(0, 3, 2, 1).reshape(44, 128, 2048))
    uwp = np.ascontiguousarray(uw.reshape(44, 128, 16, 128).transpose(0, 3, 2, 1).reshape(44, 128, 2048))
    dwp = np.ascontiguousarray(dw.reshape(16, 128, 44, 128).transpose(0, 3, 2, 1).reshape(16, 128, 5632))
    gsp = np.ascontiguousarray(gs.reshape(44, 128).T)
    usp = np.ascontiguousarray(us.reshape(44, 128).T)
    dsp = np.ascontiguousarray(ds.reshape(16, 128).T)

    MBIG = -1.0e38
    kk, qq = np.meshgrid(np.arange(128), np.arange(128), indexing="ij")
    mk = np.where(kk > qq, 0.0, MBIG).astype(np.float32)
    maskc = np.concatenate([np.zeros((128, 384), np.float32), mk,
                            np.full((128, 384), MBIG, np.float32)], axis=1)
    maskc = np.ascontiguousarray(maskc)
    ident = np.eye(128, dtype=np.float32)

    in_maps = []
    for c in range(NCORE):
        s, p = c // 2, c % 2
        xT = np.ascontiguousarray(x[s].T)
        xfm = np.ascontiguousarray(np.roll(xT, -TOWN * p, axis=1))
        pbl = np.zeros((128, 6), np.float32)
        pbl[:, 0] = 1.0 - p
        pbl[:, 1] = p / 2048.0
        pbl[:, 2] = -(B_SHIFT + DEAD * p)
        pbl[:, 4] = 1.0
        pbl[:, 5] = 1.0 - p
        in_maps.append(dict(xfm=xfm, wkp=wkp, wqp=wqp, wvp=wvp, wop=wop,
                            gwp=gwp, uwp=uwp, dwp=dwp, gsp=gsp, usp=usp,
                            dsp=dsp, maskc=maskc, ident=ident, pbl=pbl))

    nc = _get_nc()
    res = run_bass_kernel_spmd(nc, in_maps, core_ids=list(range(NCORE)))

    y = np.empty((4, S, D), dtype=np.float32)
    for s in range(4):
        fe = np.asarray(res.results[2 * s]["out"])
        fo = np.asarray(res.results[2 * s + 1]["out"])
        y[s] = np.concatenate([fe, fo], axis=1).T
    return y


# version 23 (final)
# speedup vs baseline: 1.0117x; 1.0117x over previous
import numpy as np
from contextlib import ExitStack

import concourse.bass as bass
import concourse.mybir as mybir
from concourse.bass_utils import run_bass_kernel_spmd

F32 = mybir.dt.float32
BF16 = mybir.dt.bfloat16
AF = mybir.ActivationFunctionType
ALU = mybir.AluOpType
AX = mybir.AxisListType

FMAX = float(np.finfo(np.float32).max)
B_SHIFT = 25.0
EXP_SCALE = 0.08838834764831845
DEAD = 30000.0

D = 2048
S = 2048
H = 16
DH = 128
INTER = 5632
NI = 44
NF = 16
NT = 16
TOWN = 1024
BLEND = True
DEBUG_SPILL = False
NCORE = 8


class Ctr:
    def __init__(self):
        self.v = 0

    def inc(self, n=1):
        self.v += n
        return self.v


class Plan:

    def __init__(self):
        self.ops = {k: [] for k in ("pe", "act", "dve", "gp", "sy")}

    def add(self, eng, emit, waits=(), incs=()):
        self.ops[eng].append((list(waits), emit, list(incs)))


class EngState:

    def __init__(self):
        self.waited = {}

    def run(self, e, ops):
        for waits, emit, incs in ops:
            for sem, v in waits:
                if callable(v):
                    v = v()
                if v is None:
                    continue
                k = id(sem)
                if self.waited.get(k, -1) >= v:
                    continue
                e.wait_ge(sem, v)
                self.waited[k] = v
            ins = emit(e)
            for sem, n in incs:
                ins = ins.then_inc(sem, n)


def build_nc():
    nc = bass.Bass("TRN2", target_bir_lowering=False)

    xfm = nc.dram_tensor("xfm", [D, S], F32, kind="ExternalInput").ap()
    wkp = nc.dram_tensor("wkp", [H, 128, 2048], F32, kind="ExternalInput").ap()
    wqp = nc.dram_tensor("wqp", [H, 128, 2048], F32, kind="ExternalInput").ap()
    wvp = nc.dram_tensor("wvp", [4, 128, 8192], F32, kind="ExternalInput").ap()
    wop = nc.dram_tensor("wop", [16, 128, 2048], F32, kind="ExternalInput").ap()
    gwp = nc.dram_tensor("gwp", [NI, 128, 2048], F32, kind="ExternalInput").ap()
    uwp = nc.dram_tensor("uwp", [NI, 128, 2048], F32, kind="ExternalInput").ap()
    dwp = nc.dram_tensor("dwp", [16, 128, INTER], F32, kind="ExternalInput").ap()
    gsp = nc.dram_tensor("gsp", [128, NI], F32, kind="ExternalInput").ap()
    usp = nc.dram_tensor("usp", [128, NI], F32, kind="ExternalInput").ap()
    dsp = nc.dram_tensor("dsp", [128, 16], F32, kind="ExternalInput").ap()
    maskc = nc.dram_tensor("maskc", [128, 896], F32, kind="ExternalInput").ap()
    ident = nc.dram_tensor("ident", [128, 128], F32, kind="ExternalInput").ap()
    pbl = nc.dram_tensor("pbl", [128, 6], F32, kind="ExternalInput").ap()

    out = nc.dram_tensor("out", [D, TOWN], F32, kind="ExternalOutput").ap()
    spill_kind = {"kind": "ExternalOutput"} if DEBUG_SPILL else {}
    spill = nc.dram_tensor("spill_attn", [D, TOWN], F32, **spill_kind).ap()
    if DEBUG_SPILL:
        hdump = nc.dram_tensor("hdump", [128, NI * TOWN], BF16, kind="ExternalOutput").ap()

    est = {k: EngState() for k in ("pe", "act", "dve", "gp", "sy")}

    outer = ExitStack()
    with outer:
        sPE = outer.enter_context(nc.semaphore("sPE"))
        sAC = outer.enter_context(nc.semaphore("sAC"))
        sVE = outer.enter_context(nc.semaphore("sVE"))
        sST = outer.enter_context(nc.semaphore("sST"))
        sPL = outer.enter_context(nc.semaphore("sPL"))
        cPE, cAC, cVE, cST, cPL = (Ctr() for _ in range(5))
        def dsem(name):
            sem = outer.enter_context(nc.semaphore(name))
            return [sem, Ctr()]

        semC = dsem("semC")
        semXF = [dsem(f"semXF{i}") for i in range(8)]
        semWV = dsem("semWV")
        semWK = [dsem("semWK0"), dsem("semWK1")]
        semWQ = [dsem("semWQ0"), dsem("semWQ1")]
        semWO = [dsem("semWO0"), dsem("semWO1")]
        semGW = [dsem("semGW0"), dsem("semGW1")]
        semUW = [dsem("semUW0"), dsem("semUW1")]
        semDW = [dsem("semDW0"), dsem("semDW1")]
        semSP = dsem("semSP")
        semSI = [dsem("semSI0"), dsem("semSI1")]
        semXR = [dsem("semXR0"), dsem("semXR1")]
        semOUT = [dsem("semOUT0"), dsem("semOUT1")]

        sb = lambda *a: outer.enter_context(nc.sbuf_tensor(*a))

        xn = sb("xn", [128, NF * S], BF16)
        consts = sb("consts", [128, 8], F32)
        ones_bf = sb("ones_bf", [128, 1], BF16)
        ones_row = sb("ones_row", [1, 128], F32)
        maskcb = sb("maskcb", [128, 896], BF16)
        identb = sb("identb", [128, 128], BF16)
        gsb = sb("gsb", [128, NI], F32)
        usb = sb("usb", [128, NI], F32)
        dsb = sb("dsb", [128, 16], F32)
        pbl_sb = sb("pbl_sb", [128, 6], F32)
        colvT2 = sb("colvT2", [128, 32], F32)
        colvrow = sb("colvrow", [1, 512], F32)
        btmp2 = sb("btmp2", [128, 2], F32)

        def run_block(plan):
            with nc.Block(no_gpsimd_drain=True) as block:
                @block.tensor
                def _(e):
                    est["pe"].run(e, plan.ops["pe"])

                @block.scalar
                def _(e):
                    est["act"].run(e, plan.ops["act"])

                @block.vector
                def _(e):
                    est["dve"].run(e, plan.ops["dve"])

                @block.gpsimd
                def _(e):
                    est["gp"].run(e, plan.ops["gp"])

                @block.sync
                def _(e):
                    est["sy"].run(e, plan.ops["sy"])

        att_stack = ExitStack()
        att_stack.__enter__()
        asb = lambda *a: att_stack.enter_context(nc.sbuf_tensor(*a))
        oall = asb("oall", [128, H * TOWN], BF16)
        vtm = asb("vtm", [128, NT * 512], BF16)
        wvbuf = asb("wvbuf", [128, 8192], BF16)
        wo_t = {}

        ln_stack = ExitStack()
        if True:
            lsb = lambda *a: ln_stack.enter_context(nc.sbuf_tensor(*a))
            xfmbuf = lsb("xfmbuf", [128, 2 * S], F32)
            xcb = lsb("xcb", [128, 2 * S], BF16)
            rowA = lsb("rowA", [1, S], F32)
            rowB = lsb("rowB", [1, S], F32)
            rowD = lsb("rowD", [1, S], F32)
            rowC = rowD
            rowE = rowA
            rbc = lsb("rbc", [128, S], F32)
            mrbc = lsb("mrbc", [128, S], F32)
            tmpnD = lsb("tmpnD", [128, 1024], F32)
            tmpnP = lsb("tmpnP", [128, 1024], F32)

            pl = Plan()
            pl.add("gp", (lambda e: e.dma_start(out=wvbuf[:], in_=wvp[0])),
                   incs=[(semWV[0], 16)])
            wv_t = {0: (semWV[0], semWV[1].inc(16))}
            for dst, src in ((gsb[:], gsp), (usb[:], usp),
                             (dsb[:], dsp), (pbl_sb[:], pbl)):
                pl.add("sy", (lambda e, dst=dst, src=src: e.dma_start(out=dst, in_=src)),
                       incs=[(semC[0], 16)]); semC[1].inc(16)
            pl.add("gp", (lambda e: e.dma_start(out=maskcb[:], in_=maskc)),
                   incs=[(semC[0], 16)]); semC[1].inc(16)
            pl.add("gp", (lambda e: e.dma_start(out=identb[:], in_=ident)),
                   incs=[(semC[0], 16)]); semC[1].inc(16)
            v_const_dma = (semC[0], semC[1].v)
            pl.add("dve", lambda e: e.memset(consts[:, 0:1], -B_SHIFT))
            pl.add("dve", lambda e: e.memset(consts[:, 1:2], EXP_SCALE))
            pl.add("dve", lambda e: e.memset(consts[:, 2:3], 1.0))
            pl.add("dve", lambda e: e.memset(consts[:, 3:4], 1e-5))
            pl.add("dve", lambda e: e.memset(consts[:, 4:5], -0.5))
            pl.add("dve", lambda e: e.memset(ones_bf[:], 1.0))
            pl.add("dve", lambda e: e.memset(colvT2[:], 0.0))
            pl.add("dve", lambda e: e.memset(ones_row[:], 1.0), incs=[(sVE, 1)]); cVE.inc()
            v_const_ms = cVE.v

            st_stack = ExitStack()
            s1 = st_stack.enter_context(nc.psum_tensor("s1", [1, S], F32))
            s2 = st_stack.enter_context(nc.psum_tensor("s2", [1, S], F32))
            dma_done = {}
            cast_done = {}
            sq_done = {}
            pemm_done = {}
            for fc in range(NF):
                sl = fc % 2
                w = []
                if fc >= 2:
                    w.append((sAC, sq_done[fc - 2]))
                pl.add("sy", (lambda e, sl=sl, fc=fc: e.dma_start(
                    out=xfmbuf[:, sl * S:(sl + 1) * S],
                    in_=xfm[fc * 128:(fc + 1) * 128, :])), waits=w,
                    incs=[(semXF[sl][0], 16)])
                dma_done[fc] = (semXF[sl][0], semXF[sl][1].inc(16))

                w = [dma_done[fc]]
                if fc >= 2:
                    w.append((sPE, pemm_done[fc - 2]))
                pl.add("act", (lambda e, sl=sl: e.activation(
                    xcb[:, sl * S:(sl + 1) * S], xfmbuf[:, sl * S:(sl + 1) * S], AF.Copy)),
                    waits=w, incs=[(sAC, 1)])
                cast_done[fc] = cAC.inc()

                for j in range(4):
                    w = [(sAC, cast_done[fc])] if j == 0 else []
                    if fc == 0 and j == 0:
                        w.append((sVE, v_const_ms))
                    incs = [(sPE, 1)] if j == 3 else []
                    pl.add("pe", (lambda e, sl=sl, fc=fc, j=j: e.matmul(
                        s1[0:1, j * 512:(j + 1) * 512], ones_bf[:],
                        xcb[:, sl * S + j * 512: sl * S + (j + 1) * 512],
                        start=(fc == 0), stop=(fc == NF - 1))), waits=w, incs=incs)
                v_s1 = cPE.inc()
                pl.add("act", (lambda e, sl=sl: e.activation(
                    xcb[:, sl * S:(sl + 1) * S], xcb[:, sl * S:(sl + 1) * S], AF.Square)),
                    waits=[(sPE, v_s1)], incs=[(sAC, 1)])
                sq_done[fc] = cAC.inc()
                for j in range(4):
                    w = [(sAC, sq_done[fc])] if j == 0 else []
                    incs = [(sPE, 1)] if j == 3 else []
                    pl.add("pe", (lambda e, sl=sl, fc=fc, j=j: e.matmul(
                        s2[0:1, j * 512:(j + 1) * 512], ones_bf[:],
                        xcb[:, sl * S + j * 512: sl * S + (j + 1) * 512],
                        start=(fc == 0), stop=(fc == NF - 1))), waits=w, incs=incs)
                pemm_done[fc] = cPE.inc()
            v_stats = cPE.v

            pl.add("dve", lambda e: e.tensor_copy(rowA[:], s1[:]), waits=[(sPE, v_stats)])
            pl.add("dve", lambda e: e.tensor_copy(rowB[:], s2[:]))
            pl.add("dve", lambda e: e.tensor_scalar_mul(rowA[:], rowA[:], 1.0 / D))
            pl.add("dve", lambda e: e.tensor_scalar_mul(rowB[:], rowB[:], 1.0 / D),
                   incs=[(sVE, 1)])
            v_m = cVE.inc()
            pl.add("act", lambda e: e.activation(rowC[:], rowA[:], AF.Square),
                   waits=[(sVE, v_m)], incs=[(sAC, 1)])
            v_m2 = cAC.inc()
            pl.add("dve", lambda e: e.tensor_sub(rowB[:], rowB[:], rowC[:]),
                   waits=[(sAC, v_m2)], incs=[(sVE, 1)])
            v_var = cVE.inc()
            pl.add("act", lambda e: e.activation(rowC[:], rowB[:], AF.Ln,
                                                 bias=consts[0:1, 3:4]),
                   waits=[(sVE, v_var)])
            pl.add("act", lambda e: e.activation(rowD[:], rowC[:], AF.Exp,
                                                 scale=consts[0:1, 4:5]),
                   incs=[(sAC, 1)])
            v_sd = cAC.inc()
            pl.add("dve", lambda e: e.tensor_mul(rowE[:], rowA[:], rowD[:]),
                   waits=[(sAC, v_sd)], incs=[(sVE, 1)])
            v_rows = cVE.inc()

            st_stack.close()

            a1 = ExitStack()
            a1.__enter__()
            bsb = lambda *a: a1.enter_context(nc.sbuf_tensor(*a))
            bps = lambda *a: a1.enter_context(nc.psum_tensor(*a))
            spA = bps("spA", [128, 1024], F32)
            spB = bps("spB", [128, 1024], F32)
            opsT = bps("opsT", [128, 1024], F32)
            lpsA = bps("lpsA", [128, 512], F32)
            lpsB = bps("lpsB", [128, 512], F32)
            mm_banks = [spA[:, 0:512], spA[:, 512:1024],
                        spB[:, 0:512], spB[:, 512:1024]]
            bank_wait = [[] for _ in range(4)]

            for step in range(8):
                src = rowD if step < 4 else rowE
                dst = rbc if step < 4 else mrbc
                j = step % 4
                bk = step % 2
                w = list(bank_wait[bk])
                if step == 0:
                    w.append((sVE, v_rows))
                pl.add("pe", (lambda e, bk=bk, j=j, src=src: e.matmul(
                    mm_banks[bk], ones_row[:],
                    src[0:1, j * 512:(j + 1) * 512], start=True, stop=True)),
                    waits=w, incs=[(sPE, 1)])
                vp = cPE.inc()
                pl.add("dve", (lambda e, bk=bk, j=j, dst=dst: e.tensor_copy(
                    dst[:, j * 512:(j + 1) * 512], mm_banks[bk])),
                    waits=[(sPE, vp)], incs=[(sVE, 1)])
                bank_wait[bk] = [(sVE, cVE.inc())]
            v_bc = cVE.v

            norm_done = {}
            for step in range(4 * NF):
                quarter, fc = step // NF, step % NF
                sl = step % 8
                hb = quarter * 512
                on_pool = (step % 3 == 1)
                eng = "gp" if on_pool else "dve"
                tmpn = tmpnP if on_pool else tmpnD
                sm = semXF[sl]
                w = []
                if step >= 8:
                    w.append(norm_done[step - 8])
                elif step < 2:
                    w.append((sAC, sq_done[NF - 2 + step]))
                else:
                    w.append((sAC, sq_done[NF - 1]))
                pl.add("sy", (lambda e, sl=sl, fc=fc, hb=hb: e.dma_start(
                    out=xfmbuf[:, sl * 512:(sl + 1) * 512],
                    in_=xfm[fc * 128:(fc + 1) * 128, hb:hb + 512])),
                    waits=w, incs=[(sm[0], 16)])
                d = (sm[0], sm[1].inc(16))
                pl.add(eng, (lambda e, sl=sl, hb=hb, tmpn=tmpn: e.tensor_mul(
                    tmpn[:, 0:512], xfmbuf[:, sl * 512:(sl + 1) * 512],
                    rbc[:, hb:hb + 512])), waits=[d, (sVE, v_bc)])
                inc_sem = sPL if on_pool else sVE
                ctr = cPL if on_pool else cVE
                pl.add(eng, (lambda e, sl=sl, fc=fc, hb=hb, tmpn=tmpn: e.tensor_sub(
                    xn[:, fc * S + hb: fc * S + hb + 512], tmpn[:, 0:512],
                    mrbc[:, hb:hb + 512])), incs=[(inc_sem, 1)])
                norm_done[step] = (inc_sem, ctr.inc())
            def _cover(upto):
                lastd = lastp = None
                for t in range(upto):
                    if t % 3 == 1:
                        lastp = norm_done[t]
                    else:
                        lastd = norm_done[t]
                return [x for x in (lastd, lastp) if x is not None]
            v_xn_q = [_cover(NF * (q + 1)) for q in range(4)]
            v_xn = v_xn_q[3]

            ln_stack.close()

            kh = bsb("kh", [128, S], BF16)
            wobuf = bsb("wobuf", [128, 2 * 2048], BF16)
            if True:
                qh = bsb("qh", [128, TOWN], BF16)
                pt = bsb("pt", [128, 3 * 1024], BF16)
                wkbuf = bsb("wkbuf", [128, 2 * 2048], BF16)
                wqbuf = bsb("wqbuf", [128, 2 * 2048], BF16)
                rbc_a = bsb("rbc_a", [128, 2 * 512], F32)
                addv = bsb("addv", [128, 512], F32)
                lrow = bsb("lrow", [1, 2 * 512], F32)
                rrow = bsb("rrow", [1, 2 * 512], F32)

                pair_reg = [spA[:], spB[:]]
                pt_bufs = [pt[:, 0:1024], pt[:, 1024:2048], pt[:, 2048:3072]]
                o_banks = [opsT[:, 0:512], opsT[:, 512:1024]]
                l_banks = [lpsA[0:1, :], lpsB[0:1, :]]
                l_full = [lpsA[:], lpsB[:]]
                rbc_bufs = [rbc_a[:, 0:512], rbc_a[:, 512:1024]]
                lrow_bufs = [lrow[0:1, 0:512], lrow[0:1, 512:1024]]
                rrow_bufs = [rrow[0:1, 0:512], rrow[0:1, 512:1024]]

                pt_wait = [[], [], []]
                o_wait = [[], []]
                l_wait = [[], []]

                pending = []

                def flush_pending():
                    while pending:
                        pending.pop(0)()

                pl.add("dve", (lambda e: e.memset(addv[:], 0.0)), waits=list(v_xn))

                wk_t = {}
                wq_t = {}
                wv_free = {}
                wk_free = {}
                wq_free = {}
                gp_sched = []
                for h in range(H):
                    gp_sched.append(("wk", h))
                    gp_sched.append(("wq", h))
                    if h % 4 == 1 and h // 4 + 1 < 4:
                        gp_sched.append(("wv", h // 4 + 1))

                for kind, idx in gp_sched:
                    if kind == "wv":
                        w = [(sPE, (lambda i=idx: wv_free.get(i - 1)))] if idx >= 1 else []
                        pl.add("gp", (lambda e, idx=idx: e.dma_start(
                            out=wvbuf[:], in_=wvp[idx])), waits=w, incs=[(semWV[0], 16)])
                        wv_t[idx] = (semWV[0], semWV[1].inc(16))
                    elif kind == "wk":
                        w = [(sPE, (lambda i=idx: wk_free.get(i - 2)))] if idx >= 2 else list(v_xn)
                        sm = semWK[idx % 2]
                        pl.add("gp", (lambda e, idx=idx: e.dma_start(
                            out=wkbuf[:, (idx % 2) * 2048:(idx % 2 + 1) * 2048],
                            in_=wkp[idx])), waits=w, incs=[(sm[0], 16)])
                        wk_t[idx] = (sm[0], sm[1].inc(16))
                    else:
                        w = [(sPE, (lambda i=idx: wq_free.get(i - 2)))] if idx >= 2 else list(v_xn)
                        sm = semWQ[idx % 2]
                        pl.add("gp", (lambda e, idx=idx: e.dma_start(
                            out=wqbuf[:, (idx % 2) * 2048:(idx % 2 + 1) * 2048],
                            in_=wqp[idx])), waits=w, incs=[(sm[0], 16)])
                        wq_t[idx] = (sm[0], sm[1].inc(16))

                pp = 0
                uu = 0
                for g in range(4):
                    vtm_evac = {}
                    for tau in range(NT):
                        bk = tau % 4
                        for fc in range(NF):
                            w = []
                            if fc == 0:
                                w = list(bank_wait[bk]) + [wv_t[g]] + \
                                    list(v_xn_q[min(tau // 4, 3)])
                            incs = [(sPE, 1)] if fc == NF - 1 else []
                            pl.add("pe", (lambda e, bk=bk, tau=tau, fc=fc: e.matmul(
                                mm_banks[bk],
                                xn[:, fc * S + tau * 128: fc * S + tau * 128 + 128],
                                wvbuf[:, fc * 512:(fc + 1) * 512],
                                start=(fc == 0), stop=(fc == NF - 1))), waits=w, incs=incs)
                        vmm = cPE.inc()
                        if tau == 0:
                            flush_pending()
                        pl.add("act", (lambda e, bk=bk, tau=tau: e.activation(
                            vtm[:, tau * 512:(tau + 1) * 512], mm_banks[bk], AF.Copy)),
                            waits=[(sPE, vmm)], incs=[(sAC, 1)])
                        vtm_evac[tau] = cAC.inc()
                        bank_wait[bk] = [(sAC, vtm_evac[tau])]
                    wv_free[g] = cPE.v

                    cb = 0
                    for tau in range(NT):
                        w = [(sAC, vtm_evac[tau])]
                        if tau == 0:
                            w += list(l_wait[cb])
                        incs = [(sPE, 1)] if tau == NT - 1 else []
                        pl.add("pe", (lambda e, tau=tau: e.matmul(
                            l_banks[0], ones_bf[:], vtm[:, tau * 512:(tau + 1) * 512],
                            start=(tau == 0), stop=(tau == NT - 1))), waits=w, incs=incs)
                    v_colv = cPE.inc()
                    pl.add("dve", (lambda e, g=g: e.tensor_copy(
                        colvrow[0:1, 0:512], l_banks[0])),
                        waits=[(sPE, v_colv)], incs=[(sVE, 1)])
                    v_cvrow = cVE.inc()
                    l_wait[cb] = [(sVE, v_cvrow)]
                    for j in range(4):
                        h = 4 * g + j
                        bk = j % 4
                        w = list(bank_wait[bk]) + [(sVE, v_cvrow)]
                        pl.add("pe", (lambda e, g=g, j=j, bk=bk: e.matmul(
                            mm_banks[bk][:, 0:1],
                            colvrow[0:1, j * 128:(j + 1) * 128],
                            ones_row[0:1, 0:1], start=True, stop=True)),
                            waits=w, incs=[(sPE, 1)])
                        vp = cPE.inc()
                        pl.add("dve", (lambda e, h=h, bk=bk: e.tensor_mul(
                            colvT2[:, 2 * h + 1:2 * h + 2], mm_banks[bk][:, 0:1], pbl_sb[:, 1:2])),
                            waits=[(sPE, vp), v_const_dma], incs=[(sVE, 1)])
                        bank_wait[bk] = [(sVE, cVE.inc())]

                    for j in range(4):
                        h = 4 * g + j
                        k_evac = {}
                        for blk in range(4):
                            bk = blk
                            for fc in range(NF):
                                w = []
                                if fc == 0:
                                    w = list(bank_wait[bk]) + [wk_t[h]]
                                incs = [(sPE, 1)] if fc == NF - 1 else []
                                pl.add("pe", (lambda e, h=h, bk=bk, blk=blk, fc=fc: e.matmul(
                                    mm_banks[bk],
                                    wkbuf[:, (h % 2) * 2048 + fc * 128:(h % 2) * 2048 + (fc + 1) * 128],
                                    xn[:, fc * S + blk * 512: fc * S + (blk + 1) * 512],
                                    start=(fc == 0), stop=(fc == NF - 1))), waits=w, incs=incs)
                            vmm = cPE.inc()
                            if blk == 1:
                                flush_pending()
                            pl.add("act", (lambda e, bk=bk, blk=blk: e.activation(
                                kh[:, blk * 512:(blk + 1) * 512], mm_banks[bk], AF.Copy)),
                                waits=[(sPE, vmm)], incs=[(sAC, 1)])
                            k_evac[blk] = cAC.inc()
                            bank_wait[bk] = [(sAC, k_evac[blk])]
                        wk_free[h] = cPE.v
                        q_evac = {}
                        for blk in range(2):
                            bk = blk
                            for fc in range(NF):
                                w = []
                                if fc == 0:
                                    w = list(bank_wait[bk]) + [wq_t[h]]
                                incs = [(sPE, 1)] if fc == NF - 1 else []
                                pl.add("pe", (lambda e, h=h, bk=bk, blk=blk, fc=fc: e.matmul(
                                    mm_banks[bk],
                                    wqbuf[:, (h % 2) * 2048 + fc * 128:(h % 2) * 2048 + (fc + 1) * 128],
                                    xn[:, fc * S + blk * 512: fc * S + (blk + 1) * 512],
                                    start=(fc == 0), stop=(fc == NF - 1))), waits=w, incs=incs)
                            vmm = cPE.inc()
                            pl.add("act", (lambda e, bk=bk, blk=blk: e.activation(
                                qh[:, blk * 512:(blk + 1) * 512], mm_banks[bk], AF.Copy)),
                                waits=[(sPE, vmm)], incs=[(sAC, 1)])
                            q_evac[blk] = cAC.inc()
                            bank_wait[bk] = [(sAC, q_evac[blk])]
                        wq_free[h] = cPE.v
                        v_kq_ready = cAC.v

                        units = []
                        for qb in range(2):
                            prs = [(15, 14), (13, 12), (11, 10), (9, 8),
                                   (7, 6), (5, 4)]
                            if qb == 0:
                                prs += [(3, 2), (1, 0)]
                            units.append({"qb": qb, "pairs": prs,
                                          "ou": (uu + qb) % 2, "state": []})
                        tasks = [(u, pi) for u in units
                                 for pi in range(len(u["pairs"]))]
                        ntask = len(tasks)

                        def plan_S(u, pi, gi):
                            khi, klo = u["pairs"][pi]
                            qb = u["qb"]
                            t0 = 4 * qb
                            pr = (pp + gi) % 2
                            ps3 = (pp + gi) % 3
                            bhi, blo = 2 * pr, 2 * pr + 1
                            for half, kk in ((0, khi), (1, klo)):
                                bk = bhi if half == 0 else blo
                                w = list(bank_wait[bk])
                                if gi == 0 and half == 0:
                                    w.append((sAC, v_kq_ready))
                                masked = (t0 <= kk <= t0 + 3)
                                if masked:
                                    off = (kk - t0) * 128
                                    pl.add("pe", (lambda e, bk=bk, off=off: e.matmul(
                                        mm_banks[bk], identb[:],
                                        maskcb[:, 384 - off:896 - off],
                                        start=True, stop=False)),
                                        waits=w + [v_const_dma])
                                    w = []
                                incs = [(sPE, 1)] if half == 1 else []
                                pl.add("pe", (lambda e, bk=bk, kk=kk, qb=qb, masked=masked: e.matmul(
                                    mm_banks[bk],
                                    kh[:, kk * 128:(kk + 1) * 128],
                                    qh[:, qb * 512:(qb + 1) * 512],
                                    start=(not masked), stop=True)), waits=w, incs=incs)
                            vmm = cPE.inc()
                            bias = consts[:, 0:1] if khi < 8 else pbl_sb[:, 2:3]
                            w = [(sPE, vmm), v_const_dma] + list(pt_wait[ps3])
                            pl.add("act", (lambda e, ps3=ps3, pr=pr, bias=bias: e.activation(
                                pt_bufs[ps3], pair_reg[pr], AF.Exp,
                                bias=bias, scale=consts[:, 1:2])),
                                waits=w, incs=[(sAC, 1)])
                            vexp = cAC.inc()
                            bank_wait[bhi] = [(sAC, vexp)]
                            bank_wait[blo] = [(sAC, vexp)]
                            u["state"].append((khi, klo, ps3, vexp))

                        def plan_AV(u, pi):
                            khi, klo, ps3, vexp = u["state"][pi]
                            ou = u["ou"]
                            qb = u["qb"]
                            pr = ps3
                            first = (pi == 0)
                            last = (pi == len(u["pairs"]) - 1)
                            w = [(sAC, vexp), (sAC, vtm_evac[NT - 1])]
                            if first:
                                w += list(o_wait[ou]) + list(l_wait[ou])
                            pl.add("pe", (lambda e, ou=ou, khi=khi, j=j, pr=pr, first=first: e.matmul(
                                o_banks[ou],
                                vtm[:, khi * 512 + j * 128: khi * 512 + (j + 1) * 128],
                                pt_bufs[pr][:, 0:512],
                                start=first, stop=False)), waits=w)
                            pl.add("pe", (lambda e, ou=ou, klo=klo, j=j, pr=pr, last=last: e.matmul(
                                o_banks[ou],
                                vtm[:, klo * 512 + j * 128: klo * 512 + (j + 1) * 128],
                                pt_bufs[pr][:, 512:1024],
                                start=False, stop=last)))
                            pl.add("pe", (lambda e, ou=ou, pr=pr, first=first: e.matmul(
                                l_banks[ou], ones_bf[:], pt_bufs[pr][:, 0:512],
                                start=first, stop=False)))
                            pl.add("pe", (lambda e, ou=ou, pr=pr, last=last: e.matmul(
                                l_banks[ou], ones_bf[:], pt_bufs[pr][:, 512:1024],
                                start=False, stop=last)), incs=[(sPE, 1)])
                            vav = cPE.inc()
                            pt_wait[pr] = [(sPE, vav)]
                            if not last:
                                return
                            v_last_av = cPE.v
                            pl.add("dve", (lambda e, ou=ou: e.tensor_scalar_add(
                                lrow_bufs[ou], l_banks[ou], 1e-30)),
                                waits=[(sPE, v_last_av)])
                            pl.add("dve", (lambda e, ou=ou: e.reciprocal(
                                rrow_bufs[ou], lrow_bufs[ou])), incs=[(sVE, 1)])
                            v_r = cVE.inc()
                            def mk_epilogue(ou=ou, h=h, qb=qb, v_r=v_r):
                                def ep():
                                    pl.add("pe", (lambda e: e.matmul(
                                        l_full[ou], ones_row[:], rrow_bufs[ou],
                                        start=True, stop=True)),
                                        waits=[(sVE, v_r)], incs=[(sPE, 1)])
                                    v_bcm = cPE.inc()
                                    pl.add("dve", (lambda e: e.tensor_copy(
                                        rbc_bufs[ou], l_full[ou])), waits=[(sPE, v_bcm)])
                                    pl.add("dve", (lambda e: e.tensor_mul(
                                        oall[:, h * TOWN + qb * 512: h * TOWN + (qb + 1) * 512],
                                        o_banks[ou], rbc_bufs[ou])), incs=[(sVE, 1)])
                                    v_oev = cVE.inc()
                                    o_wait[ou] = [(sVE, v_oev)]
                                    l_wait[ou] = [(sVE, v_oev)]
                                    if qb == 1 and BLEND:
                                        base = h * TOWN + 512
                                        pl.add("dve", (lambda e: e.tensor_copy(
                                            addv[:, 511:512], colvT2[:, 2 * h + 1:2 * h + 2])))
                                        pl.add("dve", (lambda e: e.tensor_add(
                                            oall[:, base:base + 512],
                                            oall[:, base:base + 512], addv[:])),
                                            incs=[(sVE, 1)])
                                        cVE.inc()
                                return ep

                            pending.append(mk_epilogue())

                        for gi, (u, pi) in enumerate(tasks):
                            plan_S(u, pi, gi)
                            if gi == 12:
                                flush_pending()
                            if gi >= 2:
                                ua, pa = tasks[gi - 2]
                                plan_AV(ua, pa)
                        ua, pa = tasks[ntask - 2]
                        plan_AV(ua, pa)
                        ua, pa = tasks[ntask - 1]
                        plan_AV(ua, pa)
                        pp += ntask
                        uu += 2
                flush_pending()
                for o in range(2):
                    sm = semWO[o % 2]
                    pl.add("gp", (lambda e, o=o: e.dma_start(
                        out=wobuf[:, (o % 2) * 2048:(o % 2 + 1) * 2048],
                        in_=wop[o])), waits=list(v_xn), incs=[(sm[0], 16)])
                    wo_t[o] = (sm[0], sm[1].inc(16))
                run_block(pl)
            a1.close()

            a2 = ExitStack()
            with a2:
                spbuf = a2.enter_context(nc.sbuf_tensor("spbuf", [128, 2 * 512], F32))
                wops = a2.enter_context(nc.psum_tensor("wops", [128, 1024], F32))
                wo_banks = [wops[:, 0:512], wops[:, 512:1024]]
                wo_bank_wait = [[], []]
                wo_free = {}
                pl = Plan()
                for o in range(2, 16):
                    w = [(sPE, (lambda o=o: wo_free.get(o - 2)))]
                    sm = semWO[o % 2]
                    pl.add("gp", (lambda e, o=o: e.dma_start(
                        out=wobuf[:, (o % 2) * 2048:(o % 2 + 1) * 2048],
                        in_=wop[o])), waits=w, incs=[(sm[0], 16)])
                    wo_t[o] = (sm[0], sm[1].inc(16))
                spw_dma = {}
                for o in range(16):
                    for qb in range(2):
                        u = 2 * o + qb
                        bk = u % 2
                        for hh in range(H):
                            w = []
                            if hh == 0:
                                w = list(wo_bank_wait[bk]) + [wo_t[o]]
                            incs = [(sPE, 1)] if hh == H - 1 else []
                            pl.add("pe", (lambda e, o=o, qb=qb, bk=bk, hh=hh: e.matmul(
                                wo_banks[bk],
                                wobuf[:, (o % 2) * 2048 + hh * 128:(o % 2) * 2048 + (hh + 1) * 128],
                                oall[:, hh * TOWN + qb * 512: hh * TOWN + (qb + 1) * 512],
                                start=(hh == 0), stop=(hh == H - 1))), waits=w, incs=incs)
                        vmm = cPE.inc()
                        wo_free[o] = vmm
                        w = [(sPE, vmm)]
                        if u >= 2:
                            w.append(spw_dma[u - 2])
                        pl.add("dve", (lambda e, bk=bk: e.tensor_copy(
                            spbuf[:, bk * 512:(bk + 1) * 512], wo_banks[bk])),
                            waits=w, incs=[(sVE, 1)])
                        vcp = cVE.inc()
                        wo_bank_wait[bk] = [(sVE, vcp)]
                        pl.add("sy", (lambda e, o=o, qb=qb, bk=bk: e.dma_start(
                            out=spill[o * 128:(o + 1) * 128, qb * 512:(qb + 1) * 512],
                            in_=spbuf[:, bk * 512:(bk + 1) * 512])),
                            waits=[(sVE, vcp)], incs=[(semSP[0], 16)])
                        spw_dma[u] = (semSP[0], semSP[1].inc(16))
                v_spill_all = (semSP[0], semSP[1].v)
                run_block(pl)
        att_stack.close()

        ffn_stack = ExitStack()
        with ffn_stack:
            fsb = lambda *a: ffn_stack.enter_context(nc.sbuf_tensor(*a))
            fps = lambda *a: ffn_stack.enter_context(nc.psum_tensor(*a))
            hbuf = fsb("hbuf", [128, NI * TOWN], BF16)
            t1b = fsb("t1b", [128, 2 * 512], BF16)
            t2b = fsb("t2b", [128, 2 * 512], BF16)
            dwbuf = fsb("dwbuf", [128, 2 * INTER], BF16)
            dw_t = {}

            f1 = ExitStack()
            with f1:
                gwbuf = f1.enter_context(nc.sbuf_tensor("gwbuf", [128, 2 * 2048], BF16))
                uwbuf = f1.enter_context(nc.sbuf_tensor("uwbuf", [128, 2 * 2048], BF16))
                gps = f1.enter_context(nc.psum_tensor("gps", [128, 1024], F32))
                ups = f1.enter_context(nc.psum_tensor("ups", [128, 1024], F32))
                g_banks = [gps[:, 0:512], gps[:, 512:1024]]
                u_banks = [ups[:, 0:512], ups[:, 512:1024]]
                g_bw = [[], []]
                u_bw = [[], []]
                t_bw = [[], []]
                gw_t = {}
                uw_t = {}
                gu_free = {}
                pl = Plan()
                for i in range(NI):
                    w = [(sPE, (lambda i=i: gu_free.get(i - 2)))] if i >= 2 else []
                    smg, smu = semGW[i % 2], semUW[i % 2]
                    pl.add("gp", (lambda e, i=i: e.dma_start(
                        out=gwbuf[:, (i % 2) * 2048:(i % 2 + 1) * 2048],
                        in_=gwp[i])), waits=w, incs=[(smg[0], 16)])
                    gw_t[i] = (smg[0], smg[1].inc(16))
                    pl.add("gp", (lambda e, i=i: e.dma_start(
                        out=uwbuf[:, (i % 2) * 2048:(i % 2 + 1) * 2048],
                        in_=uwp[i])), incs=[(smu[0], 16)])
                    uw_t[i] = (smu[0], smu[1].inc(16))
                for i in range(NI):
                    for tb in range(2):
                        u = 2 * i + tb
                        bk = u % 2
                        for fc in range(NF):
                            w = []
                            if fc == 0:
                                w = list(g_bw[bk]) + [gw_t[i]]
                            pl.add("pe", (lambda e, i=i, tb=tb, bk=bk, fc=fc: e.matmul(
                                g_banks[bk],
                                gwbuf[:, (i % 2) * 2048 + fc * 128:(i % 2) * 2048 + (fc + 1) * 128],
                                xn[:, fc * S + tb * 512: fc * S + (tb + 1) * 512],
                                start=(fc == 0), stop=(fc == NF - 1))), waits=w)
                        for fc in range(NF):
                            w = []
                            if fc == 0:
                                w = list(u_bw[bk]) + [uw_t[i]]
                            incs = [(sPE, 1)] if fc == NF - 1 else []
                            pl.add("pe", (lambda e, i=i, tb=tb, bk=bk, fc=fc: e.matmul(
                                u_banks[bk],
                                uwbuf[:, (i % 2) * 2048 + fc * 128:(i % 2) * 2048 + (fc + 1) * 128],
                                xn[:, fc * S + tb * 512: fc * S + (tb + 1) * 512],
                                start=(fc == 0), stop=(fc == NF - 1))), waits=w, incs=incs)
                        vmm = cPE.inc()
                        gu_free[i] = vmm
                        w = [(sPE, vmm), v_const_dma] + list(t_bw[bk])
                        pl.add("act", (lambda e, i=i, bk=bk: e.activation(
                            t1b[:, bk * 512:(bk + 1) * 512], g_banks[bk], AF.Silu,
                            scale=gsb[:, i:i + 1])), waits=w, incs=[(sAC, 1)])
                        cAC.inc()
                        pl.add("act", (lambda e, i=i, bk=bk: e.activation(
                            t2b[:, bk * 512:(bk + 1) * 512], u_banks[bk], AF.Copy,
                            scale=usb[:, i:i + 1])), incs=[(sAC, 1)])
                        vact = cAC.inc()
                        g_bw[bk] = [(sAC, vact)]
                        u_bw[bk] = [(sAC, vact)]
                        pl.add("dve", (lambda e, i=i, tb=tb, bk=bk: e.tensor_mul(
                            hbuf[:, i * TOWN + tb * 512: i * TOWN + (tb + 1) * 512],
                            t1b[:, bk * 512:(bk + 1) * 512],
                            t2b[:, bk * 512:(bk + 1) * 512])),
                            waits=[(sAC, vact)], incs=[(sVE, 1)])
                        t_bw[bk] = [(sVE, cVE.inc())]
                v_h_all = cVE.v
                for o in range(2):
                    sm = semDW[o % 2]
                    pl.add("gp", (lambda e, o=o: e.dma_start(
                        out=dwbuf[:, (o % 2) * INTER:(o % 2 + 1) * INTER],
                        in_=dwp[o])), incs=[(sm[0], 16)])
                    dw_t[o] = (sm[0], sm[1].inc(16))
                if DEBUG_SPILL:
                    pl.add("sy", (lambda e: e.dma_start(out=hdump, in_=hbuf[:])),
                           waits=[(sVE, v_h_all)], incs=[(sST, 16)])
                    cST.inc(16)
                run_block(pl)

            f2 = ExitStack()
            with f2:
                spin = f2.enter_context(nc.sbuf_tensor("spin", [128, 2 * 512], F32))
                xres = f2.enter_context(nc.sbuf_tensor("xres", [128, 2 * 512], F32))
                fbuf = f2.enter_context(nc.sbuf_tensor("fbuf", [128, 2 * 512], F32))
                obuf = f2.enter_context(nc.sbuf_tensor("obuf", [128, 2 * 512], F32))
                dps = f2.enter_context(nc.psum_tensor("dps", [128, 1024], F32))
                d_banks = [dps[:, 0:512], dps[:, 512:1024]]
                d_bw = [[], []]
                f_bw = [[], []]
                o_bw = [[], []]
                si_bw = [[], []]
                xr_bw = [[], []]
                dw_free = {}
                pl = Plan()
                for o in range(2, 16):
                    w = [(sPE, (lambda o=o: dw_free.get(o - 2)))]
                    sm = semDW[o % 2]
                    pl.add("gp", (lambda e, o=o: e.dma_start(
                        out=dwbuf[:, (o % 2) * INTER:(o % 2 + 1) * INTER],
                        in_=dwp[o])), waits=w, incs=[(sm[0], 16)])
                    dw_t[o] = (sm[0], sm[1].inc(16))
                sp_t = {}
                xr_t = {}
                for o in range(16):
                    for tb in range(2):
                        u = 2 * o + tb
                        bk = u % 2
                        w = [v_spill_all] + list(si_bw[bk])
                        smi, smx = semSI[bk], semXR[bk]
                        pl.add("sy", (lambda e, o=o, tb=tb, bk=bk: e.dma_start(
                            out=spin[:, bk * 512:(bk + 1) * 512],
                            in_=spill[o * 128:(o + 1) * 128, tb * 512:(tb + 1) * 512])),
                            waits=w, incs=[(smi[0], 16)])
                        sp_t[u] = (smi[0], smi[1].inc(16))
                        pl.add("sy", (lambda e, o=o, tb=tb, bk=bk: e.dma_start(
                            out=xres[:, bk * 512:(bk + 1) * 512],
                            in_=xfm[o * 128:(o + 1) * 128, tb * 512:(tb + 1) * 512])),
                            waits=list(xr_bw[bk]), incs=[(smx[0], 16)])
                        xr_t[u] = (smx[0], smx[1].inc(16))
                        for ic in range(NI):
                            w = []
                            if ic == 0:
                                w = list(d_bw[bk]) + [dw_t[o], (sVE, v_h_all)]
                            incs = [(sPE, 1)] if ic == NI - 1 else []
                            pl.add("pe", (lambda e, o=o, tb=tb, bk=bk, ic=ic: e.matmul(
                                d_banks[bk],
                                dwbuf[:, (o % 2) * INTER + ic * 128:(o % 2) * INTER + (ic + 1) * 128],
                                hbuf[:, ic * TOWN + tb * 512: ic * TOWN + (tb + 1) * 512],
                                start=(ic == 0), stop=(ic == NI - 1))), waits=w, incs=incs)
                        vmm = cPE.inc()
                        dw_free[o] = vmm
                        w = [(sPE, vmm), v_const_dma] + list(f_bw[bk])
                        pl.add("act", (lambda e, o=o, bk=bk: e.activation(
                            fbuf[:, bk * 512:(bk + 1) * 512], d_banks[bk], AF.Copy,
                            scale=dsb[:, o:o + 1])), waits=w, incs=[(sAC, 1)])
                        vact = cAC.inc()
                        d_bw[bk] = [(sAC, vact)]
                        pl.add("dve", (lambda e, bk=bk: e.tensor_add(
                            fbuf[:, bk * 512:(bk + 1) * 512],
                            fbuf[:, bk * 512:(bk + 1) * 512],
                            spin[:, bk * 512:(bk + 1) * 512])),
                            waits=[(sAC, vact), sp_t[u]], incs=[(sVE, 1)])
                        si_bw[bk] = [(sVE, cVE.inc())]
                        f_bw[bk] = list(si_bw[bk])
                        w = [xr_t[u]] + list(o_bw[bk])
                        pl.add("dve", (lambda e, bk=bk: e.tensor_add(
                            obuf[:, bk * 512:(bk + 1) * 512],
                            fbuf[:, bk * 512:(bk + 1) * 512],
                            xres[:, bk * 512:(bk + 1) * 512])),
                            waits=w, incs=[(sVE, 1)])
                        vo = cVE.inc()
                        f_bw[bk] = [(sVE, vo)]
                        xr_bw[bk] = [(sVE, vo)]
                        smo = semOUT[bk]
                        pl.add("sy", (lambda e, o=o, tb=tb, bk=bk: e.dma_start(
                            out=out[o * 128:(o + 1) * 128, tb * 512:(tb + 1) * 512],
                            in_=obuf[:, bk * 512:(bk + 1) * 512])),
                            waits=[(sVE, vo)], incs=[(smo[0], 16)])
                        o_bw[bk] = [(smo[0], smo[1].inc(16))]
                pl.add("sy", lambda e: e.nop(),
                       waits=[(semOUT[0][0], semOUT[0][1].v),
                              (semOUT[1][0], semOUT[1][1].v)])
                run_block(pl)

    return nc


_NC_CACHE = {}


def _get_nc():
    if "nc" not in _NC_CACHE:
        _NC_CACHE["nc"] = build_nc()
    return _NC_CACHE["nc"]


def kernel(L0_attn_x, L0_attn_Wq, L0_attn_Wk, L0_attn_Wv, L0_attn_Wo,
           L0_ffn_gate_w, L0_ffn_up_w, L0_ffn_down_w,
           L0_ffn_gate_s, L0_ffn_up_s, L0_ffn_down_s):
    x = np.ascontiguousarray(np.asarray(L0_attn_x, dtype=np.float32))
    wq = np.asarray(L0_attn_Wq, dtype=np.float32)
    wk = np.asarray(L0_attn_Wk, dtype=np.float32)
    wv = np.asarray(L0_attn_Wv, dtype=np.float32)
    wo = np.asarray(L0_attn_Wo, dtype=np.float32)
    gw = np.asarray(L0_ffn_gate_w, dtype=np.float32)
    uw = np.asarray(L0_ffn_up_w, dtype=np.float32)
    dw = np.asarray(L0_ffn_down_w, dtype=np.float32)
    gs = np.asarray(L0_ffn_gate_s, dtype=np.float32)
    us = np.asarray(L0_ffn_up_s, dtype=np.float32)
    ds = np.asarray(L0_ffn_down_s, dtype=np.float32)

    wkp = np.ascontiguousarray(wk.reshape(16, 128, 16, 128).transpose(2, 1, 0, 3).reshape(16, 128, 2048))
    wqp = np.ascontiguousarray(wq.reshape(16, 128, 16, 128).transpose(2, 1, 0, 3).reshape(16, 128, 2048))
    wvp = np.ascontiguousarray(wv.reshape(16, 128, 4, 512).transpose(2, 1, 0, 3).reshape(4, 128, 8192))
    wop = np.ascontiguousarray(wo.reshape(16, 128, 16, 128).transpose(2, 1, 0, 3).reshape(16, 128, 2048))
    gwp = np.ascontiguousarray(gw.reshape(44, 128, 16, 128).transpose(0, 3, 2, 1).reshape(44, 128, 2048))
    uwp = np.ascontiguousarray(uw.reshape(44, 128, 16, 128).transpose(0, 3, 2, 1).reshape(44, 128, 2048))
    dwp = np.ascontiguousarray(dw.reshape(16, 128, 44, 128).transpose(0, 3, 2, 1).reshape(16, 128, 5632))
    gsp = np.ascontiguousarray(gs.reshape(44, 128).T)
    usp = np.ascontiguousarray(us.reshape(44, 128).T)
    dsp = np.ascontiguousarray(ds.reshape(16, 128).T)

    MBIG = -1.0e38
    kk, qq = np.meshgrid(np.arange(128), np.arange(128), indexing="ij")
    mk = np.where(kk > qq, 0.0, MBIG).astype(np.float32)
    maskc = np.concatenate([np.zeros((128, 384), np.float32), mk,
                            np.full((128, 384), MBIG, np.float32)], axis=1)
    maskc = np.ascontiguousarray(maskc)
    ident = np.eye(128, dtype=np.float32)

    in_maps = []
    for c in range(NCORE):
        s, p = c // 2, c % 2
        xT = np.ascontiguousarray(x[s].T)
        xfm = np.ascontiguousarray(np.roll(xT, -TOWN * p, axis=1))
        pbl = np.zeros((128, 6), np.float32)
        pbl[:, 0] = 1.0 - p
        pbl[:, 1] = p / 2048.0
        pbl[:, 2] = -(B_SHIFT + DEAD * p)
        pbl[:, 4] = 1.0
        pbl[:, 5] = 1.0 - p
        in_maps.append(dict(xfm=xfm, wkp=wkp, wqp=wqp, wvp=wvp, wop=wop,
                            gwp=gwp, uwp=uwp, dwp=dwp, gsp=gsp, usp=usp,
                            dsp=dsp, maskc=maskc, ident=ident, pbl=pbl))

    nc = _get_nc()
    res = run_bass_kernel_spmd(nc, in_maps, core_ids=list(range(NCORE)))

    y = np.empty((4, S, D), dtype=np.float32)
    for s in range(4):
        fe = np.asarray(res.results[2 * s]["out"])
        fo = np.asarray(res.results[2 * s + 1]["out"])
        y[s] = np.concatenate([fe, fo], axis=1).T
    return y
